# version 03
# baseline (speedup 1.0000x reference)
import numpy as np
import ml_dtypes
from contextlib import ExitStack

import concourse.bass as bass
import concourse.mybir as mybir
from concourse.bass_utils import run_bass_kernel_spmd

F32 = mybir.dt.float32
BF16 = mybir.dt.bfloat16
I32 = mybir.dt.int32
AF = mybir.ActivationFunctionType
ALU = mybir.AluOpType
AX = mybir.AxisListType

D = 1024
NC = 8
DFF = 4096
NFF = 32
NTOK = 2048
NSEQ = 8192
HEADS = 16
HDIM = 64
NBF = 256
LN_EPS = 1e-5
RMS_EPS = 1e-8
PHI_EPS = 1e-6
DATA_NORM = float(np.float32(64.0 ** -0.25))
NITER_BISECT = 40
ROUTER_K = 1024.0


class Prog:
    SECTIONS = {
        "tensor": ("pe", 1),
        "vector": ("ve", 1),
        "scalar": ("sc", 1),
        "gpsimd": ("gp", 1),
    }

    def __init__(self, nc):
        self.nc = nc
        self.es = ExitStack()
        self.items = {s: [] for s in
                      ("tensor", "vector", "scalar", "gpsimd", "sync")}
        self.counts = {}
        self.n_cc = 0
        self.final_deps = []

    def _add(self, section, sem, inc, fn, deps):
        n = self.counts.get(sem, 0) + inc
        self.counts[sem] = n
        self.items[section].append((fn, [d for d in deps if d is not None],
                                    sem, inc))
        return (sem, n)

    def t(self, fn, deps=()):
        return self._add("tensor", "pe", 1, fn, deps)

    def v(self, fn, deps=()):
        return self._add("vector", "ve", 1, fn, deps)

    def s(self, fn, deps=()):
        return self._add("scalar", "sc", 1, fn, deps)

    def g(self, fn, deps=()):
        return self._add("gpsimd", "gp", 1, fn, deps)

    def sd(self, fn, deps=(), ch="ld"):
        return self._add("sync", f"syd_{ch}", 16, fn, deps)

    def ad(self, fn, deps=(), ch="ld"):
        return self._add("scalar", f"scd_{ch}", 16, fn, deps)

    def gd(self, fn, deps=(), ch="ld"):
        return self._add("gpsimd", f"gpd_{ch}", 16, fn, deps)

    def total(self, queue, ch):
        name = f"{queue}_{ch}"
        return (name, self.counts.get(name, 0))

    def col(self, fn, deps=()):
        self.n_cc += 1
        sem = f"cc{self.n_cc}"
        return self._add("gpsimd", sem, 1, fn, deps)

    def finish(self, deps):
        self.final_deps = list(deps)

    def build(self):
        nc = self.nc
        sems = {}
        for name in self.counts:
            sems[name] = self.es.enter_context(nc.semaphore(name))
        block = self.es.enter_context(nc.Block())
        hw = {}

        def coalesce(deps):
            best = {}
            for dsem, dval in deps:
                if best.get(dsem, 0) < dval:
                    best[dsem] = dval
            return best.items()

        def emit(section, eng):
            for fn, deps, sem, inc in self.items[section]:
                for dsem, dval in coalesce(deps):
                    key = (section, dsem)
                    if hw.get(key, 0) < dval:
                        hw[key] = dval
                        eng.wait_ge(sems[dsem], dval)
                ins = fn(eng)
                ins.then_inc(sems[sem], inc)
            if section == "sync":
                for dsem, dval in coalesce(self.final_deps):
                    key = (section, dsem)
                    if hw.get(key, 0) < dval:
                        hw[key] = dval
                        eng.wait_ge(sems[dsem], dval)

        if self.items["tensor"]:
            block.tensor(lambda t: emit("tensor", t))
        if self.items["vector"]:
            block.vector(lambda v: emit("vector", v))
        if self.items["scalar"]:
            block.scalar(lambda s: emit("scalar", s))
        if self.items["gpsimd"]:
            block.gpsimd(lambda g: emit("gpsimd", g))
        block.sync(lambda s: emit("sync", s))
        return self


def sb(nc, es, name, shape, dt=F32):
    return es.enter_context(nc.sbuf_tensor(name, shape, dt))


def ps(nc, es, name, shape, dt=F32):
    return es.enter_context(nc.psum_tensor(name, shape, dt))


def emit_bisect(P, s_rwf, s_rwo, s_ones2, s_oner, p_cnt, p_bc, st,
                dep_rwf, dep_rwo, dep_ident, dep_oner, dep_ones2,
                k=ROUTER_K, niter=NITER_BISECT):
    rmax, rmin = st["rmax"], st["rmin"]
    t_lo, t_hi, t_mid = st["t_lo"], st["t_hi"], st["t_mid"]
    fs, fsn, rowcnt = st["fs"], st["fsn"], st["rowcnt"]
    mask64, maskrow = st["mask64"], st["maskrow"]
    ident = st["ident"]

    a = P.v(lambda v: v.reduce_max(rmax[:], s_rwf[:], axis=AX.X), [dep_rwf])
    b = P.v(lambda v: v.tensor_reduce(rmin[:], s_rwf[:], axis=AX.X,
                                      op=ALU.min), [dep_rwf, a])
    tr1 = P.t(lambda t: t.transpose(p_bc[0:1, 0:128], rmax[:], ident[:]),
              [a, dep_ident])
    tr2 = P.t(lambda t: t.transpose(p_bc[0:1, 128:256], rmin[:], ident[:]),
              [b])
    gh = P.v(lambda v: v.tensor_reduce(st["g_hi"][:], p_bc[0:1, 0:128],
                                       axis=AX.X, op=ALU.max), [tr1, tr2])
    gl = P.v(lambda v: v.tensor_reduce(st["g_lo"][:], p_bc[0:1, 128:256],
                                       axis=AX.X, op=ALU.min), [tr2, gh])
    bh = P.t(lambda t: t.matmul(p_bc[:, 256:257], lhsT=s_oner[:],
                                rhs=st["g_hi"][:], start=True, stop=True),
             [gh, dep_oner])
    bl = P.t(lambda t: t.matmul(p_bc[:, 257:258], lhsT=s_oner[:],
                                rhs=st["g_lo"][:], start=True, stop=True),
             [gl])
    c1 = P.v(lambda v: v.tensor_copy(t_hi[:], p_bc[:, 256:257]), [bh, bl])
    c2 = P.v(lambda v: v.tensor_copy(t_lo[:], p_bc[:, 257:258]), [bl, c1])
    c3 = P.v(lambda v: v.tensor_tensor(t_mid[:], t_lo[:], t_hi[:], ALU.add),
             [c2])
    last = P.v(lambda v: v.tensor_scalar(t_mid[:], t_mid[:], 0.5, None,
                                         op0=ALU.mult), [c3])
    prev_f1 = None
    for i in range(niter):
        slot = i % 2
        mk = P.v(lambda v: v.tensor_scalar(
            mask64[:], s_rwf[:], t_mid[:], 0.0, op0=ALU.is_ge, op1=ALU.add,
            accum_out=rowcnt[:]), [last, prev_f1])
        cm = P.t(lambda t, _s=slot: t.matmul(
            p_cnt[:, _s:_s + 1], lhsT=s_ones2[:], rhs=rowcnt[:],
            start=True, stop=True), [mk, dep_ones2, prev_f1])
        f1 = P.v(lambda v, _s=slot: v.tensor_scalar(
            fs[:], p_cnt[:, _s:_s + 1], k, None, op0=ALU.is_ge), [cm])
        f2 = P.v(lambda v: v.tensor_scalar(fsn[:], fs[:], -1, 1,
                                           op0=ALU.mult, op1=ALU.add), [f1])
        s1 = P.v(lambda v: v.copy_predicated(t_lo[:], fs[:], t_mid[:]), [f2])
        s2 = P.v(lambda v: v.copy_predicated(t_hi[:], fsn[:], t_mid[:]), [s1])
        s3 = P.v(lambda v: v.tensor_tensor(t_mid[:], t_lo[:], t_hi[:],
                                           ALU.add), [s2])
        last = P.v(lambda v: v.tensor_scalar(t_mid[:], t_mid[:], 0.5, None,
                                             op0=ALU.mult), [s3])
        prev_f1 = f1
    mrow = P.v(lambda v: v.tensor_scalar(
        maskrow[:], s_rwo[:], t_lo[0:1, 0:1], None, op0=ALU.is_ge),
        [last, dep_rwo])
    return mrow


def emit_ln(P, n, i, w, s_y, s_out, s_m, s_r, s_tmp, s_trow, s_onec, s_oner,
            p_s1, p_s2, p_mbc, p_rbc, ytoks, war, s_eps=None,
            extra_out_dep=None):
    s1tok = s2tok = None
    sqprev = war.get("sq")
    for j in range(NC):
        off = j * n + i * 512
        s1tok = P.t(lambda t, _o=off, _w=w, _j=j: t.matmul(
            p_s1[:, 0:_w], lhsT=s_onec[:], rhs=s_y[:, _o:_o + _w],
            start=(_j == 0), stop=(_j == NC - 1)),
            [ytoks[j], war.get("s1")])
        sq = P.s(lambda s, _o=off, _w=w: s.activation(
            s_tmp[:, 0:_w], s_y[:, _o:_o + _w], AF.Square),
            [ytoks[j], sqprev])
        s2tok = P.t(lambda t, _w=w, _j=j: t.matmul(
            p_s2[:, 0:_w], lhsT=s_onec[:], rhs=s_tmp[:, 0:_w],
            start=(_j == 0), stop=(_j == NC - 1)), [sq, war.get("s2")])
        sqprev = s2tok
    war["sq"] = sqprev
    o = i * 512
    f1 = P.v(lambda v, _w=w, _o=o: v.tensor_scalar(
        s_m[:, _o:_o + _w], p_s1[:, 0:_w], 1.0 / D, None, op0=ALU.mult),
        [s1tok])
    f2 = P.v(lambda v, _w=w, _o=o: v.tensor_scalar(
        s_r[:, _o:_o + _w], p_s2[:, 0:_w], 1.0 / D, None, op0=ALU.mult),
        [s2tok])
    war["s1"] = f1
    war["s2"] = f2
    f3 = P.v(lambda v, _w=w, _o=o: v.tensor_tensor(
        s_trow[:, 0:_w], s_m[:, _o:_o + _w], s_m[:, _o:_o + _w],
        ALU.mult), [f1, f2, war.get("trow")])
    f4 = P.v(lambda v, _w=w, _o=o: v.tensor_tensor(
        s_r[:, _o:_o + _w], s_r[:, _o:_o + _w], s_trow[:, 0:_w],
        ALU.subtract), [f3])
    war["trow"] = f4
    f5 = P.s(lambda s, _w=w, _o=o: s.activation(
        s_r[:, _o:_o + _w], s_r[:, _o:_o + _w], AF.Sqrt,
        bias=s_eps[:]), [f4, war.get("eps_dep")])
    war["eps_dep"] = None
    f6 = P.v(lambda v, _w=w, _o=o: v.reciprocal(
        s_r[:, _o:_o + _w], s_r[:, _o:_o + _w]), [f5])
    b1 = P.t(lambda t, _w=w, _o=o: t.matmul(
        p_mbc[:, 0:_w], lhsT=s_oner[:], rhs=s_m[:, _o:_o + _w],
        start=True, stop=True), [f1, war.get("bc")])
    b2 = P.t(lambda t, _w=w, _o=o: t.matmul(
        p_rbc[:, 0:_w], lhsT=s_oner[:], rhs=s_r[:, _o:_o + _w],
        start=True, stop=True), [f6])
    outtoks = []
    for j in range(NC):
        off = j * n + i * 512
        n1 = P.v(lambda v, _o=off, _w=w: v.tensor_tensor(
            s_out[:, _o:_o + _w], s_y[:, _o:_o + _w], p_mbc[:, 0:_w],
            ALU.subtract), [b1, b2, ytoks[j], extra_out_dep])
        n2 = P.v(lambda v, _o=off, _w=w: v.tensor_tensor(
            s_out[:, _o:_o + _w], s_out[:, _o:_o + _w], p_rbc[:, 0:_w],
            ALU.mult), [n1])
        outtoks.append(n2)
    war["bc"] = outtoks[-1]
    return outtoks


def fmview(ap, n):
    return ap.rearrange("(a p) n -> p a n", p=128)


def build_a1(n=NTOK):
    nc = bass.Bass("TRN2", target_bir_lowering=False, debug=False)
    XT = nc.dram_tensor("xt", [D, n], F32, kind="ExternalInput").ap()
    RWC = nc.dram_tensor("rwc", [128, NC], F32, kind="ExternalInput").ap()
    RB = nc.dram_tensor("rb", [1, 1], F32, kind="ExternalInput").ap()
    RWS = nc.dram_tensor("rws", [1, n], F32, kind="ExternalOutput").ap()

    P = Prog(nc)
    es = P.es
    nsub = (n + 511) // 512
    s_x = sb(nc, es, "s_x", [128, NC * n])
    s_rwc = sb(nc, es, "s_rwc", [128, NC])
    s_rb = sb(nc, es, "s_rb", [1, 1])
    s_rw = sb(nc, es, "s_rw", [1, n])
    p_rw = [ps(nc, es, f"p_rw{i}", [1, 512]) for i in range(2)]

    P.sd(lambda s: s.dma_start(
        out=s_x[:].rearrange("p (a n) -> p a n", a=NC), in_=fmview(XT, n)))
    P.sd(lambda s: s.dma_start(out=s_rwc[:], in_=RWC))
    P.sd(lambda s: s.dma_start(out=s_rb[:], in_=RB))
    lall = P.total("syd", "ld")
    lx = lw = lb = lall

    evs = []
    ev_prev = [None, None]
    for i in range(nsub):
        w = min(512, n - i * 512)
        pb = p_rw[i % 2]
        mmtok = None
        for j in range(NC):
            mmtok = P.t(lambda t, _i=i, _j=j, _w=w, _pb=pb: t.matmul(
                _pb[:, 0:_w], lhsT=s_rwc[:, _j:_j + 1],
                rhs=s_x[:, _j * n + _i * 512:_j * n + _i * 512 + _w],
                start=(_j == 0), stop=(_j == NC - 1)),
                [lx, lw, ev_prev[i % 2]])
        ev = P.v(lambda v, _i=i, _w=w, _pb=pb: v.tensor_scalar(
            s_rw[:, _i * 512:_i * 512 + _w], _pb[:, 0:_w], s_rb[:], None,
            op0=ALU.add), [mmtok, lb])
        ev_prev[i % 2] = ev
        evs.append(ev)
    stok = P.sd(lambda s: s.dma_start(out=RWS, in_=s_rw[:]), evs, ch="st")
    P.finish([stok])
    return P.build().nc


def build_a2(n=NTOK, niter=NITER_BISECT):
    nc = bass.Bass("TRN2", target_bir_lowering=False, debug=False)
    XT = nc.dram_tensor("xt", [D, n], F32, kind="ExternalInput").ap()
    XRT = nc.dram_tensor("xrevt", [D, n], F32, kind="ExternalInput").ap()
    RWF = nc.dram_tensor("rwf", [128, 64], F32, kind="ExternalInput").ap()
    RWO = nc.dram_tensor("rwo", [1, n], F32, kind="ExternalInput").ap()
    IDENT = nc.dram_tensor("ident", [128, 128], F32, kind="ExternalInput").ap()
    X1T = nc.dram_tensor("x1t", [D, n], F32, kind="ExternalOutput").ap()

    P = Prog(nc)
    es = P.es
    s_x = sb(nc, es, "s_x", [128, NC * n])
    s_xr = sb(nc, es, "s_xr", [128, NC * n])
    s_y = s_xr
    s_rwf = sb(nc, es, "s_rwf", [128, 64])
    s_rwo = sb(nc, es, "s_rwo", [1, n])
    s_ones2 = sb(nc, es, "s_ones2", [128, 128])
    s_oner = sb(nc, es, "s_oner", [1, 128])
    s_onec = sb(nc, es, "s_onec", [128, 1])
    s_ident = sb(nc, es, "s_ident", [128, 128])
    st = {kk: sb(nc, es, kk, [128, 1]) for kk in
          ("rmax", "rmin", "t_lo", "t_hi", "t_mid", "rowcnt")}
    st["fs"] = sb(nc, es, "fs", [128, 1], I32)
    st["fsn"] = sb(nc, es, "fsn", [128, 1], I32)
    st["g_hi"] = sb(nc, es, "g_hi", [1, 1])
    st["g_lo"] = sb(nc, es, "g_lo", [1, 1])
    st["mask64"] = sb(nc, es, "mask64", [128, 64])
    st["maskrow"] = sb(nc, es, "maskrow", [1, n])
    st["ident"] = s_ident
    s_m = sb(nc, es, "s_m", [1, n])
    s_r = sb(nc, es, "s_r", [1, n])
    s_tmp = sb(nc, es, "s_tmp", [128, 512])
    s_trow = sb(nc, es, "s_trow", [1, 512])
    s_eps = sb(nc, es, "s_eps", [1, 1])
    p_cnt = ps(nc, es, "p_cnt", [128, 8])
    p_bc = ps(nc, es, "p_bc", [128, 512])
    p_s1 = ps(nc, es, "p_s1", [1, 512])
    p_s2 = ps(nc, es, "p_s2", [1, 512])
    p_mbc = ps(nc, es, "p_mbc", [128, 512])
    p_rbc = ps(nc, es, "p_rbc", [128, 512])
    p_msk = ps(nc, es, "p_msk", [128, 512])

    P.sd(lambda s: s.dma_start(
        out=s_x[:].rearrange("p (a n) -> p a n", a=NC), in_=fmview(XT, n)))
    P.sd(lambda s: s.dma_start(
        out=s_xr[:].rearrange("p (a n) -> p a n", a=NC), in_=fmview(XRT, n)))
    P.sd(lambda s: s.dma_start(out=s_rwf[:], in_=RWF))
    P.sd(lambda s: s.dma_start(out=s_rwo[:], in_=RWO))
    P.sd(lambda s: s.dma_start(out=s_ident[:], in_=IDENT))
    lall = P.total("syd", "ld")
    lx = lxr = lrwf = lrwo = lid = lall
    m1 = P.v(lambda v: v.memset(s_ones2[:], 1.0))
    m2 = P.v(lambda v: v.memset(s_oner[:], 1.0))
    m3 = P.v(lambda v: v.memset(s_onec[:], 1.0))
    m4 = P.v(lambda v: v.memset(s_eps[:], LN_EPS))

    mrow = emit_bisect(P, s_rwf, s_rwo, s_ones2, s_oner, p_cnt, p_bc, st,
                       lrwf, lrwo, lid, m2, m1, niter=niter)

    nsub = (n + 511) // 512
    stoks = []
    war = {"eps_dep": m4}
    prev_msk = None
    for i in range(nsub):
        w = min(512, n - i * 512)
        mb = P.t(lambda t, _i=i, _w=w: t.matmul(
            p_msk[:, 0:_w], lhsT=s_oner[:],
            rhs=st["maskrow"][:, _i * 512:_i * 512 + _w],
            start=True, stop=True), [mrow, m2, prev_msk])
        ytoks = []
        for j in range(NC):
            off = j * n + i * 512
            ym = P.v(lambda v, _o=off, _w=w: v.tensor_tensor(
                s_y[:, _o:_o + _w], s_xr[:, _o:_o + _w], p_msk[:, 0:_w],
                ALU.mult), [mb, lxr])
            ya = P.v(lambda v, _o=off, _w=w: v.tensor_tensor(
                s_y[:, _o:_o + _w], s_y[:, _o:_o + _w], s_x[:, _o:_o + _w],
                ALU.add), [ym, lx])
            ytoks.append(ya)
        prev_msk = ytoks[-1]
        outtoks = emit_ln(P, n, i, w, s_y, s_x, s_m, s_r, s_tmp, s_trow,
                          s_onec, s_oner, p_s1, p_s2, p_mbc, p_rbc, ytoks, war,
                          s_eps=s_eps)
        stok = P.sd(lambda s, _i=i, _w=w: s.dma_start(
            out=fmview(X1T, n)[:, :, _i * 512:_i * 512 + _w],
            in_=s_x[:].rearrange("p (a n) -> p a n", a=NC)[
                :, :, _i * 512:_i * 512 + _w]), outtoks, ch="st")
        stoks.append(stok)
    P.finish(stoks)
    return P.build().nc


def build_b1():
    nc = bass.Bass("TRN2", target_bir_lowering=False, debug=False)
    WU = nc.dram_tensor("wupt_s", [D, 512], F32, kind="ExternalInput").ap()
    WD = nc.dram_tensor("wdnt_s", [DFF, 128], F32, kind="ExternalInput").ap()
    AP_ = nc.dram_tensor("aparts", [1, 2], F32, kind="ExternalOutput").ap()

    P = Prog(nc)
    es = P.es
    s_wu = sb(nc, es, "s_wu", [128, NC * 512])
    s_wd = sb(nc, es, "s_wd", [128, NFF * 128])
    s_acc = sb(nc, es, "s_acc", [128, NC + NFF])
    s_red = sb(nc, es, "s_red", [128, 2])
    s_onec = sb(nc, es, "s_onec", [128, 1])
    s_out = sb(nc, es, "s_out", [1, 2])
    p_o = ps(nc, es, "p_o", [1, 8])

    P.sd(lambda s: s.dma_start(
        out=s_wu[:].rearrange("p (a n) -> p a n", a=NC),
        in_=WU.rearrange("(a p) n -> p a n", p=128)))
    P.sd(lambda s: s.dma_start(
        out=s_wd[:].rearrange("p (a n) -> p a n", a=NFF),
        in_=WD.rearrange("(a p) n -> p a n", p=128)))
    lall = P.total("syd", "ld")
    m1 = P.v(lambda v: v.memset(s_onec[:], 1.0))

    rtoks = []
    for j in range(NC):
        rtoks.append(P.v(lambda v, _j=j: v.tensor_reduce(
            s_acc[:, _j:_j + 1], s_wu[:, _j * 512:(_j + 1) * 512],
            axis=AX.X, op=ALU.add, apply_absolute_value=True), [lall]))
    for j in range(NFF):
        rtoks.append(P.v(lambda v, _j=j: v.tensor_reduce(
            s_acc[:, NC + _j:NC + _j + 1], s_wd[:, _j * 128:(_j + 1) * 128],
            axis=AX.X, op=ALU.add, apply_absolute_value=True), [lall]))
    r1 = P.v(lambda v: v.tensor_reduce(
        s_red[:, 0:1], s_acc[:, 0:NC], axis=AX.X, op=ALU.add), rtoks)
    r2 = P.v(lambda v: v.tensor_reduce(
        s_red[:, 1:2], s_acc[:, NC:NC + NFF], axis=AX.X, op=ALU.add), rtoks)
    mm = P.t(lambda t: t.matmul(p_o[0:1, 0:2], lhsT=s_onec[:],
                                rhs=s_red[:, 0:2], start=True, stop=True),
             [r1, r2, m1])
    ev = P.v(lambda v: v.tensor_copy(s_out[:], p_o[0:1, 0:2]), [mm])
    stok = P.sd(lambda s: s.dma_start(out=AP_, in_=s_out[:]), [ev], ch="st")
    P.finish([stok])
    return P.build().nc


def build_b1b():
    nc = bass.Bass("TRN2", target_bir_lowering=False, debug=False)
    WU = nc.dram_tensor("wupt_s", [D, 512], F32, kind="ExternalInput").ap()
    WD = nc.dram_tensor("wdnt_s", [DFF, 128], F32, kind="ExternalInput").ap()
    APS = nc.dram_tensor("aparts8", [1, 16], F32, kind="ExternalInput").ap()
    WTU = nc.dram_tensor("wtu_s", [D, 512], BF16, kind="ExternalOutput").ap()
    WTD = nc.dram_tensor("wtd_s", [DFF, 128], BF16, kind="ExternalOutput").ap()

    P = Prog(nc)
    es = P.es
    s_wu = sb(nc, es, "s_wu", [128, NC * 512])
    s_wd = sb(nc, es, "s_wd", [128, NFF * 128])
    s_aps = sb(nc, es, "s_aps", [1, 16])
    s_th = sb(nc, es, "s_th", [1, 2])
    s_thc = sb(nc, es, "s_thc", [128, 2])
    s_oner = sb(nc, es, "s_oner", [1, 128])
    s_msk = sb(nc, es, "s_msk", [128, 512])
    s_sgn = sb(nc, es, "s_sgn", [128, 512])
    s_wtu = sb(nc, es, "s_wtu", [128, NC * 512], BF16)
    s_wtd = sb(nc, es, "s_wtd", [128, NFF * 128], BF16)
    p_b = ps(nc, es, "p_b", [128, 8])

    P.sd(lambda s: s.dma_start(
        out=s_wu[:].rearrange("p (a n) -> p a n", a=NC),
        in_=WU.rearrange("(a p) n -> p a n", p=128)))
    P.sd(lambda s: s.dma_start(
        out=s_wd[:].rearrange("p (a n) -> p a n", a=NFF),
        in_=WD.rearrange("(a p) n -> p a n", p=128)))
    P.sd(lambda s: s.dma_start(out=s_aps[:], in_=APS))
    lall = P.total("syd", "ld")
    m1 = P.v(lambda v: v.memset(s_oner[:], 1.0))
    a1 = P.v(lambda v: v.tensor_reduce(
        s_th[:, 0:1], s_aps[0:1, 0:16:2], axis=AX.X, op=ALU.add), [lall])
    a2 = P.v(lambda v: v.tensor_reduce(
        s_th[:, 1:2], s_aps[0:1, 1:16:2], axis=AX.X, op=ALU.add), [lall, a1])
    a3 = P.v(lambda v: v.tensor_scalar(
        s_th[:, 0:2], s_th[:, 0:2], 0.5 / (D * DFF), None, op0=ALU.mult),
        [a1, a2])
    bb = P.t(lambda t: t.matmul(p_b[:, 0:2], lhsT=s_oner[:], rhs=s_th[:, 0:2],
                                start=True, stop=True), [a3, m1])
    cc = P.v(lambda v: v.tensor_copy(s_thc[:], p_b[:, 0:2]), [bb])

    stoks = []
    prevm = prevs = None
    for which, s_w, s_wt, nch, wid, OUT in (
            (0, s_wu, s_wtu, NC, 512, WTU),
            (1, s_wd, s_wtd, NFF, 128, WTD)):
        for j in range(nch):
            sl = slice(j * wid, (j + 1) * wid)
            ab = P.s(lambda s, _sl=sl, _wd=wid, _sw=s_w: s.activation(
                s_msk[:, 0:_wd], _sw[:, _sl], AF.Abs), [lall, prevm])
            mk = P.v(lambda v, _w=which, _wd=wid: v.tensor_scalar(
                s_msk[:, 0:_wd], s_msk[:, 0:_wd], s_thc[:, _w:_w + 1], None,
                op0=ALU.is_gt), [cc, ab, prevm])
            sg = P.s(lambda s, _sl=sl, _wd=wid, _sw=s_w: s.activation(
                s_sgn[:, 0:_wd], _sw[:, _sl], AF.Sign), [lall, prevs, ab])
            wt = P.v(lambda v, _sl=sl, _wd=wid, _swt=s_wt: v.tensor_tensor(
                _swt[:, _sl], s_msk[:, 0:_wd], s_sgn[:, 0:_wd], ALU.mult),
                [mk, sg])
            prevm = prevs = wt
        stoks.append(P.sd(lambda s, _swt=s_wt, _n=nch, _wd=wid, _O=OUT:
                          s.dma_start(
            out=_O.rearrange("(a p) n -> p a n", p=128),
            in_=_swt[:].rearrange("p (a n) -> p a n", a=_n)),
            [prevm], ch="st"))
    P.finish(stoks)
    return P.build().nc


def build_b2(n=NTOK):
    nc = bass.Bass("TRN2", target_bir_lowering=False, debug=False)
    X1T = nc.dram_tensor("x1t", [D, n], F32, kind="ExternalInput").ap()
    WTU = nc.dram_tensor("wtu", [D, DFF], BF16, kind="ExternalInput").ap()
    WTD = nc.dram_tensor("wtd", [DFF, D], BF16, kind="ExternalInput").ap()
    X2T = nc.dram_tensor("x2t", [D, n], F32, kind="ExternalOutput").ap()

    P = Prog(nc)
    es = P.es
    s_x1 = [sb(nc, es, f"s_x1{i}", [128, NC * 512]) for i in range(2)]
    s_xb = sb(nc, es, "s_xb", [128, NC * 512], BF16)
    s_h = sb(nc, es, "s_h", [128, NFF * 512])
    s_hb = sb(nc, es, "s_hb", [128, NFF * 512], BF16)
    s_z = sb(nc, es, "s_z", [128, NC * 512])
    s_wu = [sb(nc, es, f"s_wu{i}", [128, NC * 128], BF16) for i in range(2)]
    s_wd = [sb(nc, es, f"s_wd{i}", [128, NFF * 128], BF16) for i in range(2)]
    s_tmp = sb(nc, es, "s_tmp", [128, 512])
    s_row = sb(nc, es, "s_row", [1, 512])
    s_hrow = sb(nc, es, "s_hrow", [1, 512])
    s_m = sb(nc, es, "s_m", [1, 512])
    s_r = sb(nc, es, "s_r", [1, 512])
    s_trow = sb(nc, es, "s_trow", [1, 512])
    s_eps = sb(nc, es, "s_eps", [1, 1])
    s_onec = sb(nc, es, "s_onec", [128, 1])
    s_oner = sb(nc, es, "s_oner", [1, 128])
    p_up = [ps(nc, es, f"p_up{i}", [128, 512]) for i in range(2)]
    p_dn = [ps(nc, es, f"p_dn{i}", [128, 512]) for i in range(2)]
    p_rms = ps(nc, es, "p_rms", [1, 512])
    p_h2 = ps(nc, es, "p_h2", [1, 512])
    p_bc1 = ps(nc, es, "p_bc1", [128, 512])
    p_bc2 = ps(nc, es, "p_bc2", [128, 512])

    lall = P.total("syd", "ld")
    m1 = P.v(lambda v: v.memset(s_onec[:], 1.0))
    m2 = P.v(lambda v: v.memset(s_oner[:], 1.0))
    m4 = P.v(lambda v: v.memset(s_eps[:], LN_EPS))

    nsub = (n + 511) // 512
    war = {"eps_dep": m4}
    wtoks = {}
    wconsumed = {}
    stoks = []
    xls = [None, None]
    xls[0] = P.sd(lambda s: s.dma_start(
        out=s_x1[0][:].rearrange("p (a n) -> p a n", a=NC),
        in_=fmview(X1T, n)[:, :, 0:512]), ch="x0")
    for i in range(nsub):
        w = min(512, n - i * 512)
        xb = i % 2
        if i + 1 < nsub:
            w2_ = min(512, n - (i + 1) * 512)
            xls[1 - xb] = P.sd(lambda s, _i=i, _b=1 - xb, _w2=w2_:
                               s.dma_start(
                out=s_x1[_b][:].rearrange("p (a n) -> p a n", a=NC)[
                    :, :, 0:_w2],
                in_=fmview(X1T, n)[:, :, (_i + 1) * 512:(_i + 1) * 512 + _w2]),
                [war.get(("x1free", 1 - xb))], ch=f"x{1 - xb}")
        lall = xls[xb]
        sqprev = war.get("sq")
        s1tok = None
        for j in range(NC):
            off = j * n + i * 512
            sq = P.s(lambda s, _o=j * 512, _w=w, _b=xb: s.activation(
                s_tmp[:, 0:_w], s_x1[_b][:, _o:_o + _w], AF.Square),
                [lall, sqprev])
            s1tok = P.t(lambda t, _w=w, _j=j: t.matmul(
                p_rms[:, 0:_w], lhsT=s_onec[:], rhs=s_tmp[:, 0:_w],
                start=(_j == 0), stop=(_j == NC - 1)),
                [sq, m1, war.get("rms")])
            sqprev = s1tok
        war["sq"] = sqprev
        r1 = P.v(lambda v, _w=w: v.tensor_scalar(
            s_row[:, 0:_w], p_rms[:, 0:_w], 1.0 / D, None, op0=ALU.mult),
            [s1tok, war.get("row")])
        war["rms"] = r1
        r2 = P.s(lambda s, _w=w: s.activation(
            s_row[:, 0:_w], s_row[:, 0:_w], AF.Sqrt), [r1])
        r3 = P.v(lambda v, _w=w: v.tensor_scalar(
            s_row[:, 0:_w], s_row[:, 0:_w], RMS_EPS, None, op0=ALU.add), [r2])
        r4 = P.v(lambda v, _w=w: v.reciprocal(
            s_row[:, 0:_w], s_row[:, 0:_w]), [r3])
        b1 = P.t(lambda t, _w=w: t.matmul(
            p_bc1[:, 0:_w], lhsT=s_oner[:], rhs=s_row[:, 0:_w],
            start=True, stop=True), [r4, m2, war.get("bc1")])
        casts = []
        for j in range(NC):
            off = j * n + i * 512
            casts.append(P.v(lambda v, _j=j, _w=w, _b=xb: v.tensor_tensor(
                s_xb[:, _j * 512:_j * 512 + _w],
                s_x1[_b][:, _j * 512:_j * 512 + _w],
                p_bc1[:, 0:_w], ALU.mult), [b1, lall, war.get("xb")]))
        war["row"] = casts[-1]

        hsq_prev = war.get("sq")
        h2tok = None
        geltoks = []
        for f in range(NFF):
            buf = f % 2
            wtok = P.sd(lambda s, _f=f, _b=buf: s.dma_start(
                out=s_wu[_b][:].rearrange("p (a n) -> p a n", a=NC),
                in_=WTU.rearrange("(a p) f -> p a f", p=128)[
                    :, :, _f * 128:(_f + 1) * 128]),
                [wconsumed.get(("wu", buf))], ch=f"wu{buf}")
            mmt = None
            for j in range(NC):
                mmt = P.t(lambda t, _f=f, _j=j, _b=buf, _w=w: t.matmul(
                    p_up[_f % 2][:, 0:_w],
                    lhsT=s_wu[_b][:, _j * 128:(_j + 1) * 128],
                    rhs=s_xb[:, _j * 512:_j * 512 + _w],
                    start=(_j == 0), stop=(_j == NC - 1)),
                    [wtok, casts[j], war.get(("up", f % 2))])
            wconsumed[("wu", buf)] = mmt
            gel = P.s(lambda s, _f=f, _w=w: s.activation(
                s_h[:, _f * 512:_f * 512 + _w], p_up[_f % 2][:, 0:_w],
                AF.Gelu), [mmt, war.get(("hread", f))])
            war[("up", f % 2)] = gel
            geltoks.append(gel)
            hsq = P.s(lambda s, _f=f, _w=w: s.activation(
                s_tmp[:, 0:_w], s_h[:, _f * 512:_f * 512 + _w], AF.Square),
                [gel, hsq_prev])
            h2tok = P.t(lambda t, _f=f, _w=w: t.matmul(
                p_h2[:, 0:_w], lhsT=s_onec[:], rhs=s_tmp[:, 0:_w],
                start=(_f == 0), stop=(_f == NFF - 1)),
                [hsq, war.get("h2")])
            hsq_prev = h2tok
        war["sq"] = hsq_prev
        hr1 = P.v(lambda v, _w=w: v.tensor_scalar(
            s_hrow[:, 0:_w], p_h2[:, 0:_w], 1.0 / DFF, None, op0=ALU.mult),
            [h2tok, war.get("hrow")])
        war["h2"] = hr1
        hr2 = P.s(lambda s, _w=w: s.activation(
            s_hrow[:, 0:_w], s_hrow[:, 0:_w], AF.Sqrt), [hr1])
        hr3 = P.v(lambda v, _w=w: v.tensor_scalar(
            s_hrow[:, 0:_w], s_hrow[:, 0:_w], RMS_EPS, None, op0=ALU.add),
            [hr2])
        hr4 = P.v(lambda v, _w=w: v.reciprocal(
            s_hrow[:, 0:_w], s_hrow[:, 0:_w]), [hr3])
        b2t = P.t(lambda t, _w=w: t.matmul(
            p_bc2[:, 0:_w], lhsT=s_oner[:], rhs=s_hrow[:, 0:_w],
            start=True, stop=True), [hr4, war.get("bc2")])
        hcasts = []
        for f in range(NFF):
            hcasts.append(P.v(lambda v, _f=f, _w=w: v.tensor_tensor(
                s_hb[:, _f * 512:_f * 512 + _w],
                s_h[:, _f * 512:_f * 512 + _w], p_bc2[:, 0:_w], ALU.mult),
                [b2t, geltoks[f], war.get("hb")]))
        war["hrow"] = hcasts[-1]

        ztoks = []
        for cjj in range(NC):
            buf = cjj % 2
            wtok = P.sd(lambda s, _c=cjj, _b=buf: s.dma_start(
                out=s_wd[_b][:].rearrange("p (a n) -> p a n", a=NFF),
                in_=WTD.rearrange("(a p) c -> p a c", p=128)[
                    :, :, _c * 128:(_c + 1) * 128]),
                [wconsumed.get(("wd", buf))], ch=f"wd{buf}")
            mmt = None
            for f in range(NFF):
                mmt = P.t(lambda t, _c=cjj, _f=f, _b=buf, _w=w: t.matmul(
                    p_dn[_c % 2][:, 0:_w],
                    lhsT=s_wd[_b][:, _f * 128:(_f + 1) * 128],
                    rhs=s_hb[:, _f * 512:_f * 512 + _w],
                    start=(_f == 0), stop=(_f == NFF - 1)),
                    [wtok, hcasts[f], war.get(("dn", cjj % 2))])
            wconsumed[("wd", buf)] = mmt
            zt = P.v(lambda v, _c=cjj, _w=w, _b=xb: v.tensor_tensor(
                s_z[:, _c * 512:_c * 512 + _w], p_dn[_c % 2][:, 0:_w],
                s_x1[_b][:, _c * 512:_c * 512 + _w], ALU.add),
                [mmt, lall, war.get(("zread", cjj))])
            war[("dn", cjj % 2)] = zt
            ztoks.append(zt)
        war["hb"] = ztoks[-1]
        war["xb"] = ztoks[-1]

        outtoks = emit_ln(P, 512, 0, w, s_z, s_z, s_m, s_r, s_tmp, s_trow,
                          s_onec, s_oner, p_rms[0:1, :], p_h2[0:1, :],
                          p_bc1, p_bc2, ztoks, war, s_eps=s_eps)
        war["rms"] = war["s1"]
        war["h2"] = war["s2"]
        war["bc1"] = war["bc"]
        war["bc2"] = war["bc"]
        for f in range(NFF):
            war[("hread", f)] = hcasts[f]
        stoks.append(P.sd(lambda s, _i=i, _w=w: s.dma_start(
            out=fmview(X2T, n)[:, :, _i * 512:_i * 512 + _w],
            in_=s_z[:].rearrange("p (a n) -> p a n", a=NC)[:, :, 0:_w]),
            outtoks, ch="st"))
        for cjj in range(NC):
            war[("zread", cjj)] = stoks[-1]
        war[("x1free", xb)] = ztoks[-1]
    P.finish(stoks)
    return P.build().nc


def build_c1(ns=NSEQ):
    nc = bass.Bass("TRN2", target_bir_lowering=False, debug=False)
    X2 = nc.dram_tensor("x2tb", [D, ns], F32, kind="ExternalInput").ap()
    WQ = nc.dram_tensor("wq", [D, 256], F32, kind="ExternalInput").ap()
    WK = nc.dram_tensor("wk", [D, 256], F32, kind="ExternalInput").ap()
    WV = nc.dram_tensor("wv", [D, 256], F32, kind="ExternalInput").ap()
    BQ = nc.dram_tensor("bq", [128, 2], F32, kind="ExternalInput").ap()
    BK = nc.dram_tensor("bk", [128, 2], F32, kind="ExternalInput").ap()
    BV = nc.dram_tensor("bvrow", [1, 256], F32, kind="ExternalInput").ap()
    PJ = nc.dram_tensor("projt", [64, 4 * 256], F32, kind="ExternalInput").ap()
    IDB = nc.dram_tensor("identb", [128, 128], BF16, kind="ExternalInput").ap()
    ND = nc.dram_tensor("ndt", [256, ns], BF16, kind="ExternalOutput").ap()

    P = Prog(nc)
    es = P.es
    s_xs = [sb(nc, es, f"s_xs{i}", [128, NC * 512], BF16) for i in range(2)]
    s_q = [sb(nc, es, f"s_q{i}", [128, ns], BF16) for i in range(2)]
    s_k = [sb(nc, es, f"s_k{i}", [128, ns], BF16) for i in range(2)]
    s_wq = sb(nc, es, "s_wq", [128, NC * 256], BF16)
    s_wk = sb(nc, es, "s_wk", [128, NC * 256], BF16)
    s_wv = sb(nc, es, "s_wv", [128, NC * 256], BF16)
    s_bq = sb(nc, es, "s_bq", [128, 2])
    s_bk = sb(nc, es, "s_bk", [128, 2])
    s_bv = sb(nc, es, "s_bv", [1, 256])
    s_pjr = sb(nc, es, "s_pjr", [128, 4 * 256])
    s_pj = sb(nc, es, "s_pj", [128, 4 * 256], BF16)
    s_idb = sb(nc, es, "s_idb", [128, 128], BF16)
    s_oner = sb(nc, es, "s_oner", [1, 128])
    s_vx = sb(nc, es, "s_vx", [128, (ns // 128) * 260], BF16)
    s_phik = sb(nc, es, "s_phik", [128, 256], BF16)
    s_exp = sb(nc, es, "s_exp", [128, 256])
    s_nmx = sb(nc, es, "s_nmx", [128, 1])
    s_kv = sb(nc, es, "s_kv", [128, 2 * 260])
    s_kvb = sb(nc, es, "s_kvb", [128, 2 * 260], BF16)
    s_phiqT = [sb(nc, es, f"s_phiqT{i}", [128, 512], BF16) for i in range(2)]
    s_den = sb(nc, es, "s_den", [1, 512])
    s_dbc = sb(nc, es, "s_dbc", [64, 512])
    s_nd = [sb(nc, es, f"s_nd{i}", [128, ns], BF16) for i in range(2)]
    p_a = ps(nc, es, "p_a", [128, 512])
    p_b = ps(nc, es, "p_b", [128, 512])
    p_c = ps(nc, es, "p_c", [128, 512])
    p_x = [ps(nc, es, f"p_x{i}", [128, 512]) for i in range(2)]
    p_kv = [ps(nc, es, f"p_kv{i}", [128, 512]) for i in range(2)]
    p_t = ps(nc, es, "p_t", [128, 1024], BF16)

    lw1 = P.gd(lambda g: g.dma_start(
        out=s_wq[:].rearrange("p (a n) -> p a n", a=NC),
        in_=WQ.rearrange("(a p) n -> p a n", p=128)), ch="w")
    lw2 = P.gd(lambda g: g.dma_start(
        out=s_wk[:].rearrange("p (a n) -> p a n", a=NC),
        in_=WK.rearrange("(a p) n -> p a n", p=128)), ch="w")
    lw3 = P.gd(lambda g: g.dma_start(
        out=s_wv[:].rearrange("p (a n) -> p a n", a=NC),
        in_=WV.rearrange("(a p) n -> p a n", p=128)), ch="w")
    lg = P.total("gpd", "w")
    P.sd(lambda s: s.dma_start(out=s_bq[:], in_=BQ))
    P.sd(lambda s: s.dma_start(out=s_bk[:], in_=BK))
    P.sd(lambda s: s.dma_start(out=s_bv[:], in_=BV))
    P.sd(lambda s: s.dma_start(out=s_idb[:], in_=IDB))
    P.sd(lambda s: s.dma_start(out=s_pjr[0:64, :], in_=PJ))
    P.sd(lambda s: s.dma_start(out=s_pjr[64:128, :], in_=PJ))
    lall = P.total("syd", "ld")
    m1 = P.v(lambda v: v.memset(s_oner[:], 1.0))
    m2 = P.v(lambda v: v.memset(s_vx[:], 1.0))
    pj = P.v(lambda v: v.tensor_scalar(
        s_pj[:], s_pjr[:], DATA_NORM, None, op0=ALU.mult), [lall])

    nsub = ns // 512
    war = {}
    xload = [None, None]
    xload[0] = P.gd(lambda g: g.dma_start(
        out=s_xs[0][:].rearrange("p (a n) -> p a n", a=NC),
        in_=fmview(X2, ns)[:, :, 0:512]), ch="xs0")

    for i in range(nsub):
        buf = i % 2
        if i + 1 < nsub:
            xload[1 - buf] = P.gd(lambda g, _i=i, _b=1 - buf: g.dma_start(
                out=s_xs[_b][:].rearrange("p (a n) -> p a n", a=NC),
                in_=fmview(X2, ns)[:, :, (_i + 1) * 512:(_i + 2) * 512]),
                [war.get(("xread", 1 - buf))], ch=f"xs{1 - buf}")
        lx = xload[buf]
        pq = [p_a, p_b]
        for W_, Sdst, Bias in ((s_wq, s_q, s_bq), (s_wk, s_k, s_bk)):
            for hp in range(2):
                mmt = None
                for j in range(NC):
                    mmt = P.t(lambda t, _j=j, _hp=hp, _b=buf, _W=W_: t.matmul(
                        pq[_hp][:, 0:512],
                        lhsT=_W[:, _j * 256 + _hp * 128:
                                _j * 256 + _hp * 128 + 128],
                        rhs=s_xs[_b][:, _j * 512:(_j + 1) * 512],
                        start=(_j == 0), stop=(_j == NC - 1)),
                        [lx, lg, war.get(("pq", hp))])
                ev = P.s(lambda s, _i=i, _hp=hp, _S=Sdst, _B=Bias:
                         s.activation(
                    _S[_hp][:, _i * 512:_i * 512 + 512], pq[_hp][:, 0:512],
                    AF.Identity, bias=_B[:, _hp:_hp + 1]), [mmt, lall])
                war[("pq", hp)] = ev
                if W_ is s_wk:
                    war[("kwrite", hp)] = ev
        lastv = None
        for r in range(4):
            rr = r * 128
            tt = i * 4 + r
            mmt = None
            for j in range(NC):
                mmt = P.t(lambda t, _j=j, _rr=rr, _b=buf: t.matmul(
                    p_c[:, 0:256],
                    lhsT=s_xs[_b][:, _j * 512 + _rr:_j * 512 + _rr + 128],
                    rhs=s_wv[:, _j * 256:(_j + 1) * 256],
                    start=(_j == 0), stop=False),
                    [lx, war.get("vread")])
            mmt = P.t(lambda t: t.matmul(
                p_c[:, 0:256], lhsT=s_oner[:], rhs=s_bv[:],
                start=False, stop=True), [mmt, m1, lall])
            vx = P.v(lambda v, _tt=tt: v.tensor_copy(
                s_vx[:, _tt * 260:(_tt + 1) * 260].rearrange(
                    "p (h c) -> p h c", h=4)[:, :, 0:64],
                p_c[:, 0:256].rearrange("p (h c) -> p h c", h=4)),
                [mmt, m2])
            war["vread"] = vx
            lastv = vx
        war[("xread", buf)] = lastv

    kvevs = []
    for h in range(4):
        hp, hb = h // 2, (h % 2) * 64
        kv0 = kv1 = None
        for tt in range(ns // 128):
            rr = tt * 128
            xp = P.t(lambda t, _hp=hp, _hb=hb, _rr=rr, _tt=tt, _h=h:
                     t.matmul(
                p_x[_tt % 2][:, 0:256],
                lhsT=s_k[_hp][_hb:_hb + 64, _rr:_rr + 128],
                rhs=s_pj[_hb:_hb + 64, _h * 256:(_h + 1) * 256],
                start=True, stop=True),
                [war.get(("kwrite", hp)), pj, war.get(("xp", tt % 2)),
                 war.get("vread")])
            rm = P.v(lambda v, _tt=tt: v.reduce_max(
                s_nmx[:], p_x[_tt % 2][:, 0:256], axis=AX.X, negate=True),
                [xp, war.get("nmxread")])
            ex = P.s(lambda s, _tt=tt: s.activation(
                s_exp[:], p_x[_tt % 2][:, 0:256], AF.Exp, bias=s_nmx[:]),
                [rm, war.get("exread")])
            war[("xp", tt % 2)] = ex
            war["nmxread"] = ex
            ph = P.v(lambda v: v.tensor_scalar(
                s_phik[:], s_exp[:], 1.0 / 16.0, PHI_EPS / 16.0,
                op0=ALU.mult, op1=ALU.add), [ex, war.get("phikread")])
            war["exread"] = ph
            kv0 = P.t(lambda t, _h=h, _tt=tt: t.matmul(
                p_kv[0][:, 0:65], lhsT=s_phik[:, 0:128],
                rhs=s_vx[:, _tt * 260 + _h * 65:_tt * 260 + (_h + 1) * 65],
                start=(_tt == 0), stop=(_tt == ns // 128 - 1)),
                [ph, war.get("kv0ev")])
            kv1 = P.t(lambda t, _h=h, _tt=tt: t.matmul(
                p_kv[1][:, 0:65], lhsT=s_phik[:, 128:256],
                rhs=s_vx[:, _tt * 260 + _h * 65:_tt * 260 + (_h + 1) * 65],
                start=(_tt == 0), stop=(_tt == ns // 128 - 1)),
                [ph, war.get("kv1ev")])
            war["phikread"] = kv1
        e1 = P.v(lambda v, _h=h: v.tensor_copy(
            s_kv[:, _h * 65:(_h + 1) * 65], p_kv[0][:, 0:65]), [kv0, kv1])
        e2 = P.v(lambda v, _h=h: v.tensor_copy(
            s_kv[:, 260 + _h * 65:260 + (_h + 1) * 65], p_kv[1][:, 0:65]),
            [e1])
        war["kv0ev"] = e1
        war["kv1ev"] = e2
        kvevs.append(e2)
    cb = P.v(lambda v: v.tensor_copy(s_kvb[:], s_kv[:]), kvevs)

    ndlast = None
    for i in range(nsub):
        for h in range(4):
            hp, hb = h // 2, (h % 2) * 64
            trtoks = []
            for r in range(4):
                n0 = i * 512 + r * 128
                xp = P.t(lambda t, _hp=hp, _hb=hb, _n0=n0, _r=r, _h=h:
                         t.matmul(
                    p_x[_r % 2][:, 0:256],
                    lhsT=s_q[_hp][_hb:_hb + 64, _n0:_n0 + 128],
                    rhs=s_pj[_hb:_hb + 64, _h * 256:(_h + 1) * 256],
                    start=True, stop=True), [pj, war.get(("xp", r % 2))])
                rm = P.v(lambda v, _r=r: v.reduce_max(
                    s_nmx[:], p_x[_r % 2][:, 0:256], axis=AX.X, negate=True),
                    [xp, war.get("nmxread")])
                ex = P.s(lambda s, _r=r: s.activation(
                    s_exp[:], p_x[_r % 2][:, 0:256], AF.Exp, bias=s_nmx[:]),
                    [rm, war.get("exread")])
                war[("xp", r % 2)] = ex
                war["nmxread"] = ex
                ph = P.v(lambda v: v.tensor_scalar(
                    s_phik[:], s_exp[:], 1.0 / 16.0, PHI_EPS / 16.0,
                    op0=ALU.mult, op1=ALU.add), [ex, war.get("phikread")])
                war["exread"] = ph
                t1 = P.t(lambda t: t.transpose(
                    p_t[:, 0:128], s_phik[:, 0:128], s_idb[:]),
                    [ph, lall, war.get("tr0")])
                t2 = P.t(lambda t: t.transpose(
                    p_t[:, 128:256], s_phik[:, 128:256], s_idb[:]), [t1])
                c1t = P.v(lambda v, _r=r: v.tensor_copy(
                    s_phiqT[0][:, _r * 128:(_r + 1) * 128], p_t[:, 0:128]),
                    [t2, war.get("pTread")])
                c2t = P.v(lambda v, _r=r: v.tensor_copy(
                    s_phiqT[1][:, _r * 128:(_r + 1) * 128], p_t[:, 128:256]),
                    [c1t])
                war["phikread"] = c2t
                war["tr0"] = c2t
                trtoks.append(c2t)
            nm = P.t(lambda t, _h=h: t.matmul(
                p_a[0:65, 0:512], lhsT=s_kvb[:, _h * 65:(_h + 1) * 65],
                rhs=s_phiqT[0][:], start=True, stop=False),
                trtoks + [cb, war.get("numread")])
            nm = P.t(lambda t, _h=h: t.matmul(
                p_a[0:65, 0:512],
                lhsT=s_kvb[:, 260 + _h * 65:260 + (_h + 1) * 65],
                rhs=s_phiqT[1][:], start=False, stop=True), [nm])
            war["pTread"] = nm
            d1 = P.v(lambda v: v.tensor_scalar(
                s_den[:], p_a[64:65, 0:512], PHI_EPS, None, op0=ALU.add),
                [nm, war.get("denread")])
            d2 = P.v(lambda v: v.reciprocal(s_den[:], s_den[:]), [d1])
            db = P.t(lambda t: t.matmul(
                p_b[0:64, 0:512], lhsT=s_oner[:, 0:64], rhs=s_den[:],
                start=True, stop=True), [d2, m1, war.get("dbcread")])
            dc = P.v(lambda v: v.tensor_copy(s_dbc[:], p_b[0:64, 0:512]),
                     [db])
            war["denread"] = dc
            nd = P.v(lambda v, _hp=hp, _hb=hb, _i=i: v.tensor_tensor(
                s_nd[_hp][_hb:_hb + 64, _i * 512:_i * 512 + 512],
                p_a[0:64, 0:512], s_dbc[:], ALU.mult), [dc, nm])
            war["numread"] = nd
            war["dbcread"] = nd
            ndlast = nd
    stoks = []
    for hp in range(2):
        stoks.append(P.sd(lambda s, _hp=hp: s.dma_start(
            out=ND[_hp * 128:(_hp + 1) * 128, :], in_=s_nd[_hp][:]),
            [ndlast], ch="st"))
    P.finish(stoks)
    return P.build().nc


def build_c2(n=NTOK):
    nc = bass.Bass("TRN2", target_bir_lowering=False, debug=False)
    NDS = nc.dram_tensor("ndts", [D, n], BF16, kind="ExternalInput").ap()
    WO = nc.dram_tensor("wo", [D, D], F32, kind="ExternalInput").ap()
    BO = nc.dram_tensor("bo", [128, NC], F32, kind="ExternalInput").ap()
    X2T = nc.dram_tensor("x2t", [D, n], F32, kind="ExternalInput").ap()
    X3T = nc.dram_tensor("x3t", [D, n], F32, kind="ExternalOutput").ap()

    P = Prog(nc)
    es = P.es
    s_ndv = sb(nc, es, "s_ndv", [128, NC * n], BF16)
    s_x2 = sb(nc, es, "s_x2", [128, NC * n])
    s_wo = sb(nc, es, "s_wo", [128, NC * D], BF16)
    s_bo = sb(nc, es, "s_bo", [128, NC])
    s_z = sb(nc, es, "s_z", [128, NC * 512])
    s_tmp = sb(nc, es, "s_tmp", [128, 512])
    s_m = sb(nc, es, "s_m", [1, 512])
    s_r = sb(nc, es, "s_r", [1, 512])
    s_trow = sb(nc, es, "s_trow", [1, 512])
    s_eps = sb(nc, es, "s_eps", [1, 1])
    s_onec = sb(nc, es, "s_onec", [128, 1])
    s_oner = sb(nc, es, "s_oner", [1, 128])
    p_o = [ps(nc, es, f"p_o{i}", [128, 512]) for i in range(2)]
    p_s1 = ps(nc, es, "p_s1", [1, 512])
    p_s2 = ps(nc, es, "p_s2", [1, 512])
    p_mbc = ps(nc, es, "p_mbc", [128, 512])
    p_rbc = ps(nc, es, "p_rbc", [128, 512])

    P.sd(lambda s: s.dma_start(
        out=s_ndv[:].rearrange("p (a n) -> p a n", a=NC), in_=fmview(NDS, n)))
    P.sd(lambda s: s.dma_start(
        out=s_x2[:].rearrange("p (a n) -> p a n", a=NC), in_=fmview(X2T, n)))
    P.sd(lambda s: s.dma_start(out=s_bo[:], in_=BO))
    lall = P.total("syd", "ld")
    lwo = P.gd(lambda g: g.dma_start(
        out=s_wo[:].rearrange("p (a n) -> p a n", a=NC),
        in_=WO.rearrange("(a p) n -> p a n", p=128)), ch="w")
    m1 = P.v(lambda v: v.memset(s_onec[:], 1.0))
    m2 = P.v(lambda v: v.memset(s_oner[:], 1.0))
    m4 = P.v(lambda v: v.memset(s_eps[:], LN_EPS))

    nsub = (n + 511) // 512
    war = {"eps_dep": m4}
    stoks = []
    for i in range(nsub):
        w = min(512, n - i * 512)
        ztoks = []
        for cj in range(NC):
            mmt = None
            for j in range(NC):
                mmt = P.t(lambda t, _c=cj, _j=j, _i=i, _w=w: t.matmul(
                    p_o[_c % 2][:, 0:_w],
                    lhsT=s_wo[:, _j * D + _c * 128:_j * D + (_c + 1) * 128],
                    rhs=s_ndv[:, _j * n + _i * 512:_j * n + _i * 512 + _w],
                    start=(_j == 0), stop=(_j == NC - 1)),
                    [lall, lwo, war.get(("o", cj % 2))])
            ev = P.s(lambda s, _c=cj, _w=w: s.activation(
                s_tmp[:, 0:_w], p_o[_c % 2][:, 0:_w], AF.Identity,
                bias=s_bo[:, _c:_c + 1]), [mmt, war.get("tmpread")])
            zt = P.v(lambda v, _c=cj, _i=i, _w=w: v.tensor_tensor(
                s_z[:, _c * 512:_c * 512 + _w], s_tmp[:, 0:_w],
                s_x2[:, _c * n + _i * 512:_c * n + _i * 512 + _w], ALU.add),
                [ev, war.get(("zread", cj))])
            war[("o", cj % 2)] = zt
            war["tmpread"] = zt
            ztoks.append(zt)
        war["sq"] = ztoks[-1]
        outtoks = emit_ln(P, 512, 0, w, s_z, s_z, s_m, s_r, s_tmp, s_trow,
                          s_onec, s_oner, p_s1, p_s2, p_mbc, p_rbc, ztoks,
                          war, s_eps=s_eps)
        war["tmpread"] = war["sq"]
        stoks.append(P.sd(lambda s, _i=i, _w=w: s.dma_start(
            out=fmview(X3T, n)[:, :, _i * 512:_i * 512 + _w],
            in_=s_z[:].rearrange("p (a n) -> p a n", a=NC)[:, :, 0:_w]),
            outtoks, ch="st"))
        for cj in range(NC):
            war[("zread", cj)] = stoks[-1]
    P.finish(stoks)
    return P.build().nc


def build_d(n=NTOK):
    nc = bass.Bass("TRN2", target_bir_lowering=False, debug=False)
    X3T = nc.dram_tensor("x3t", [D, n], F32, kind="ExternalInput").ap()
    WIN = nc.dram_tensor("win", [D, 64], F32, kind="ExternalInput").ap()
    BIN = nc.dram_tensor("bin", [64, 1], F32, kind="ExternalInput").ap()
    W1 = nc.dram_tensor("w1", [64, 16], F32, kind="ExternalInput").ap()
    B1 = nc.dram_tensor("b1", [64, 16], F32, kind="ExternalInput").ap()
    W2 = nc.dram_tensor("w2", [64, 16], F32, kind="ExternalInput").ap()
    B2_ = nc.dram_tensor("b2", [64, 1], F32, kind="ExternalInput").ap()
    WOUT = nc.dram_tensor("wout", [64, D], F32, kind="ExternalInput").ap()
    BOUT = nc.dram_tensor("bout", [128, NC], F32, kind="ExternalInput").ap()
    YT = nc.dram_tensor("yt", [D, n], F32, kind="ExternalOutput").ap()

    P = Prog(nc)
    es = P.es
    s_x3 = sb(nc, es, "s_x3", [128, NC * n])
    s_x3b = sb(nc, es, "s_x3b", [128, NC * n], BF16)
    s_winb = sb(nc, es, "s_winb", [128, NC * 64], BF16)
    s_bin = sb(nc, es, "s_bin", [64, 1])
    s_w1 = sb(nc, es, "s_w1", [64, 16])
    s_b1 = sb(nc, es, "s_b1", [64, 16])
    s_w2 = sb(nc, es, "s_w2", [64, 16])
    s_b2 = sb(nc, es, "s_b2", [64, 1])
    s_woutb = sb(nc, es, "s_woutb", [64, D], BF16)
    s_bout = sb(nc, es, "s_bout", [128, NC])
    s_s = sb(nc, es, "s_s", [64, 512])
    s_hi = sb(nc, es, "s_hi", [64, 512])
    s_g = sb(nc, es, "s_g", [64, 512])
    s_gb = sb(nc, es, "s_gb", [64, 512], BF16)
    s_z = sb(nc, es, "s_z", [128, NC * 512])
    s_tmp = sb(nc, es, "s_tmp", [128, 512])
    s_m = sb(nc, es, "s_m", [1, 512])
    s_r = sb(nc, es, "s_r", [1, 512])
    s_trow = sb(nc, es, "s_trow", [1, 512])
    s_eps = sb(nc, es, "s_eps", [1, 1])
    s_onec = sb(nc, es, "s_onec", [128, 1])
    s_oner = sb(nc, es, "s_oner", [1, 128])
    p_s = ps(nc, es, "p_s", [128, 512])
    p_o = [ps(nc, es, f"p_o{i}", [128, 512]) for i in range(2)]
    p_s1 = ps(nc, es, "p_s1", [1, 512])
    p_s2 = ps(nc, es, "p_s2", [1, 512])
    p_mbc = ps(nc, es, "p_mbc", [128, 512])
    p_rbc = ps(nc, es, "p_rbc", [128, 512])

    P.sd(lambda s: s.dma_start(
        out=s_x3[:].rearrange("p (a n) -> p a n", a=NC), in_=fmview(X3T, n)))
    P.sd(lambda s: s.dma_start(out=s_bin[:], in_=BIN))
    P.sd(lambda s: s.dma_start(out=s_w1[:], in_=W1))
    P.sd(lambda s: s.dma_start(out=s_b1[:], in_=B1))
    P.sd(lambda s: s.dma_start(out=s_w2[:], in_=W2))
    P.sd(lambda s: s.dma_start(out=s_b2[:], in_=B2_))
    P.sd(lambda s: s.dma_start(out=s_bout[:], in_=BOUT))
    lall = P.total("syd", "ld")
    lx3b = P.gd(lambda g: g.dma_start(
        out=s_x3b[:].rearrange("p (a n) -> p a n", a=NC), in_=fmview(X3T, n)),
        ch="w")
    lwin = P.gd(lambda g: g.dma_start(
        out=s_winb[:].rearrange("p (a n) -> p a n", a=NC),
        in_=WIN.rearrange("(a p) n -> p a n", p=128)), ch="w")
    lwout = P.gd(lambda g: g.dma_start(out=s_woutb[:], in_=WOUT), ch="w")
    lgw = P.total("gpd", "w")
    m1 = P.v(lambda v: v.memset(s_onec[:], 1.0))
    m2 = P.v(lambda v: v.memset(s_oner[:], 1.0))
    m4 = P.v(lambda v: v.memset(s_eps[:], LN_EPS))

    nsub = (n + 511) // 512
    war = {"eps_dep": m4}
    stoks = []
    for i in range(nsub):
        w = min(512, n - i * 512)
        mmt = None
        for j in range(NC):
            mmt = P.t(lambda t, _j=j, _i=i, _w=w: t.matmul(
                p_s[0:64, 0:_w], lhsT=s_winb[:, _j * 64:(_j + 1) * 64],
                rhs=s_x3b[:, _j * n + _i * 512:_j * n + _i * 512 + _w],
                start=(_j == 0), stop=(_j == NC - 1)),
                [lgw, war.get("psread")])
        sv = P.s(lambda s, _w=w: s.activation(
            s_s[:, 0:_w], p_s[0:64, 0:_w], AF.Identity, bias=s_bin[:]),
            [mmt, lall, war.get("sread")])
        war["psread"] = sv
        gt = None
        for ii in range(16):
            hi = P.s(lambda s, _ii=ii, _w=w: s.activation(
                s_hi[:, 0:_w], s_s[:, 0:_w], AF.Relu,
                scale=s_w1[:, _ii:_ii + 1], bias=s_b1[:, _ii:_ii + 1]),
                [sv, lall, war.get("hiread")])
            if ii == 0:
                gt = P.v(lambda v, _ii=ii, _w=w: v.tensor_scalar(
                    s_g[:, 0:_w], s_hi[:, 0:_w], s_w2[:, _ii:_ii + 1], None,
                    op0=ALU.mult), [hi, war.get("gread")])
            else:
                gt = P.v(lambda v, _ii=ii, _w=w: v.scalar_tensor_tensor(
                    s_g[:, 0:_w], s_hi[:, 0:_w], s_w2[:, _ii:_ii + 1],
                    s_g[:, 0:_w], op0=ALU.mult, op1=ALU.add), [hi, gt])
            war["hiread"] = gt
        war["sread"] = gt
        g2 = P.v(lambda v, _w=w: v.tensor_scalar(
            s_g[:, 0:_w], s_g[:, 0:_w], s_b2[:], None, op0=ALU.add), [gt])
        gb = P.v(lambda v, _w=w: v.tensor_copy(
            s_gb[:, 0:_w], s_g[:, 0:_w]), [g2, war.get("gbread")])
        war["gread"] = gb
        ztoks = []
        for cj in range(NC):
            mmt = P.t(lambda t, _c=cj, _w=w: t.matmul(
                p_o[_c % 2][:, 0:_w],
                lhsT=s_woutb[:, _c * 128:(_c + 1) * 128],
                rhs=s_gb[:, 0:_w], start=True, stop=True),
                [gb, lgw, war.get(("o", cj % 2))])
            ev = P.s(lambda s, _c=cj, _w=w: s.activation(
                s_tmp[:, 0:_w], p_o[_c % 2][:, 0:_w], AF.Identity,
                bias=s_bout[:, _c:_c + 1]), [mmt, war.get("tmpread")])
            zt = P.v(lambda v, _c=cj, _i=i, _w=w: v.tensor_tensor(
                s_z[:, _c * 512:_c * 512 + _w], s_tmp[:, 0:_w],
                s_x3[:, _c * n + _i * 512:_c * n + _i * 512 + _w], ALU.add),
                [ev, lall, war.get(("zread", cj))])
            war[("o", cj % 2)] = zt
            war["tmpread"] = zt
            ztoks.append(zt)
        war["gbread"] = ztoks[-1]
        war["sq"] = ztoks[-1]
        outtoks = emit_ln(P, 512, 0, w, s_z, s_z, s_m, s_r, s_tmp, s_trow,
                          s_onec, s_oner, p_s1, p_s2, p_mbc, p_rbc, ztoks,
                          war, s_eps=s_eps)
        war["tmpread"] = war["sq"]
        stoks.append(P.sd(lambda s, _i=i, _w=w: s.dma_start(
            out=fmview(YT, n)[:, :, _i * 512:_i * 512 + _w],
            in_=s_z[:].rearrange("p (a n) -> p a n", a=NC)[:, :, 0:_w]),
            outtoks, ch="st"))
        for cj in range(NC):
            war[("zread", cj)] = stoks[-1]
    P.finish(stoks)
    return P.build().nc


def _run(nc, maps):
    res = run_bass_kernel_spmd(nc, maps, core_ids=list(range(8)))
    return res.results


def kernel(L0_attn_x, L0_attn_router_w, L0_attn_router_b, L0_attn_wq,
           L0_attn_wk, L0_attn_wv, L0_ffn_w_up, L0_ffn_w_down, L1_attn_w_qkv,
           L1_attn_b_qkv, L1_attn_w_out, L1_attn_b_out, L1_attn_proj_matrix,
           L1_ffn_Win, L1_ffn_bin_, L1_ffn_W1, L1_ffn_b1, L1_ffn_W2,
           L1_ffn_b2, L1_ffn_Wout, L1_ffn_bout):
    x = np.ascontiguousarray(L0_attn_x, dtype=np.float32)
    ident = np.eye(128, dtype=np.float32)
    identb = np.eye(128, dtype=np.float32).astype(ml_dtypes.bfloat16)

    def core_bo(core):
        return core // 4, (core % 4) * NTOK

    xt = {}
    for core in range(8):
        b, o = core_bo(core)
        xt[core] = np.ascontiguousarray(x[b, o:o + NTOK].T)

    nc = build_a1()
    rwc = np.ascontiguousarray(L0_attn_router_w.reshape(NC, 128).T)
    rb = np.array([[L0_attn_router_b[0]]], dtype=np.float32)
    maps = [{"xt": xt[c], "rwc": rwc, "rb": rb} for c in range(8)]
    r = _run(nc, maps)
    rw_full = np.zeros((2, NSEQ), np.float32)
    for core in range(8):
        b, o = core_bo(core)
        rw_full[b, o:o + NTOK] = r[core]["rws"][0]

    nc = build_a2()
    maps = []
    for core in range(8):
        b, o = core_bo(core)
        maps.append({
            "xt": xt[core],
            "xrevt": np.ascontiguousarray(x[b][::-1][o:o + NTOK].T),
            "rwf": np.ascontiguousarray(rw_full[b].reshape(128, 64)),
            "rwo": np.ascontiguousarray(rw_full[b][o:o + NTOK][None, :]),
            "ident": ident,
        })
    r = _run(nc, maps)
    x1t = {c: r[c]["x1t"] for c in range(8)}

    wupt = np.ascontiguousarray(L0_ffn_w_up.T)
    wdnt = np.ascontiguousarray(L0_ffn_w_down.T)
    nc = build_b1()
    maps = [{"wupt_s": np.ascontiguousarray(wupt[:, 512 * c:512 * (c + 1)]),
             "wdnt_s": np.ascontiguousarray(wdnt[:, 128 * c:128 * (c + 1)])}
            for c in range(8)]
    r = _run(nc, maps)
    aparts = np.concatenate([r[c]["aparts"] for c in range(8)], axis=1)
    nc = build_b1b()
    for c in range(8):
        maps[c]["aparts8"] = aparts
    r = _run(nc, maps)
    wtu = np.ascontiguousarray(
        np.concatenate([r[c]["wtu_s"] for c in range(8)], axis=1))
    wtd = np.ascontiguousarray(
        np.concatenate([r[c]["wtd_s"] for c in range(8)], axis=1))

    nc = build_b2()
    maps = [{"x1t": x1t[c], "wtu": wtu, "wtd": wtd} for c in range(8)]
    r = _run(nc, maps)
    x2t = {c: r[c]["x2t"] for c in range(8)}
    x2b_full = {b: np.concatenate([x2t[b * 4 + g] for g in range(4)], axis=1)
                for b in range(2)}

    nc = build_c1()
    wqkv = L1_attn_w_qkv
    maps = []
    for core in range(8):
        b, g = core // 4, core % 4
        cols = slice(g * 256, (g + 1) * 256)
        maps.append({
            "x2tb": x2b_full[b],
            "wq": np.ascontiguousarray(wqkv[:, 0:D][:, cols]),
            "wk": np.ascontiguousarray(wqkv[:, D:2 * D][:, cols]),
            "wv": np.ascontiguousarray(wqkv[:, 2 * D:3 * D][:, cols]),
            "bq": np.ascontiguousarray(
                L1_attn_b_qkv[0:D][cols].reshape(2, 128).T),
            "bk": np.ascontiguousarray(
                L1_attn_b_qkv[D:2 * D][cols].reshape(2, 128).T),
            "bvrow": np.ascontiguousarray(
                L1_attn_b_qkv[2 * D:3 * D][cols][None, :]),
            "projt": np.ascontiguousarray(np.concatenate(
                [L1_attn_proj_matrix[4 * g + h].T for h in range(4)], axis=1)),
            "identb": identb,
        })
    r = _run(nc, maps)
    nd_full = {b: np.concatenate([r[b * 4 + g]["ndt"] for g in range(4)],
                                 axis=0) for b in range(2)}

    nc = build_c2()
    maps = []
    for core in range(8):
        b, o = core_bo(core)
        maps.append({
            "ndts": np.ascontiguousarray(nd_full[b][:, o:o + NTOK]),
            "wo": L1_attn_w_out,
            "bo": np.ascontiguousarray(L1_attn_b_out.reshape(NC, 128).T),
            "x2t": x2t[core],
        })
    r = _run(nc, maps)
    x3t = {c: r[c]["x3t"] for c in range(8)}

    nc = build_d()
    maps = []
    for core in range(8):
        maps.append({
            "x3t": x3t[core],
            "win": L1_ffn_Win,
            "bin": np.ascontiguousarray(L1_ffn_bin_[:, None]),
            "w1": L1_ffn_W1, "b1": L1_ffn_b1, "w2": L1_ffn_W2,
            "b2": np.ascontiguousarray(L1_ffn_b2[:, None]),
            "wout": L1_ffn_Wout,
            "bout": np.ascontiguousarray(L1_ffn_bout.reshape(NC, 128).T),
        })
    r = _run(nc, maps)
    out = np.zeros((2, NSEQ, D), np.float32)
    for core in range(8):
        b, o = core_bo(core)
        out[b, o:o + NTOK] = r[core]["yt"].T
    return out
